# version 09
# baseline (speedup 1.0000x reference)
import numpy as np
import concourse.bass as bass
import concourse.mybir as mybir
from concourse.bass_utils import run_bass_kernel_spmd

F32 = mybir.dt.float32
BF16 = mybir.dt.bfloat16
AX = mybir.AxisListType.X
OP = mybir.AluOpType
AF = mybir.ActivationFunctionType

NCORES = 8
SEQS = 2
T = 2048
D = 1024
HQ, HKV, HD = 16, 4, 64
E, H = 8, 2048
NT = T // 128
DCH = D // 128
HCH = H // 128
EPS = 1e-5
NEGBIG = -1.0e30

ENGS = ("t", "v", "s", "g", "y")
SEMS = ("dX", "dO", "tS", "tC", "vS", "sS", "gS",
        "wA", "wB", "wG", "u1a", "u1b", "u3a", "u3b", "u2")

HEADS_A = [0, 1, 2, 3, 8, 9, 10, 11]
HEADS_B = [4, 5, 6, 7, 12, 13, 14, 15]
HEAD_PERM = []
for _c in range(8):
    HEAD_PERM.append(HEADS_A[_c])
    HEAD_PERM.append(HEADS_B[_c])

PHASES = ("attn", "ffn")
DEBUG_SKIP = set()
DEBUG_HQ = None
DEBUG_NQI = None


MAX_MM_INFLIGHT = 3


class Plan:
    def __init__(self):
        self.ops = {e: [] for e in ENGS}
        self.cnt = {s: 0 for s in SEMS}
        self.lastwait = {}

    def n(self, sem):
        return self.cnt[sem]

    def emit(self, eng, method, args=(), kwargs=None, waits=(), inc=None, incby=1):
        if eng == "t" and method == "matmul":
            waits = tuple(waits) + (("tS", self.cnt["tS"] + 1 - MAX_MM_INFLIGHT),)
            real = []
            for snm, val in waits:
                if val > 0 and self.lastwait.get((eng, snm), -1) < val:
                    real.append((snm, val))
                    self.lastwait[(eng, snm)] = val
            self.ops[eng].append((method, args, kwargs or {}, tuple(real), None, 0))
            self.cnt["tS"] += 1
            self.ops[eng].append(("nop", (), {}, (), "tS", 1))
            return self.cnt["tS"]
        if eng == "v" and method is not None:
            waits = tuple(waits) + (("vS", self.cnt["vS"]),)
            inc, incby = "vS", 1
        elif eng == "s" and method is not None:
            waits = tuple(waits) + (("sS", self.cnt["sS"]),)
            inc, incby = "sS", 1
        real = []
        for snm, val in waits:
            if val > 0 and self.lastwait.get((eng, snm), -1) < val:
                real.append((snm, val))
                self.lastwait[(eng, snm)] = val
        if inc is not None:
            self.cnt[inc] += incby
        self.ops[eng].append((method, args, kwargs or {}, tuple(real), inc, incby))
        return self.cnt[inc] if inc is not None else 0

    def barrier(self):
        self.emit("t", "nop", inc="tS")
        self.emit("v", "nop", inc="vS")
        self.emit("s", "nop", inc="sS")
        self.emit("g", "nop", inc="gS")
        snap = [(s, self.cnt[s]) for s in SEMS]
        for e in ENGS:
            self.emit(e, None, waits=snap)


def build_program(phases=None):
    phases = PHASES if phases is None else phases
    nc = bass.Bass("TRN2", target_bir_lowering=False)

    x_hbm = nc.dram_tensor("xin", [SEQS * T, D], F32, kind="ExternalInput")
    y_hbm = nc.dram_tensor("yout", [SEQS * T, D], F32, kind="ExternalOutput")
    wq_h = nc.dram_tensor("wq", [128, 8192], F32, kind="ExternalInput")
    wk_h = nc.dram_tensor("wk", [128, 2048], F32, kind="ExternalInput")
    wv_h = nc.dram_tensor("wv", [128, 2048], F32, kind="ExternalInput")
    wo_h = nc.dram_tensor("wo", [128, 8192], F32, kind="ExternalInput")
    wg_h = nc.dram_tensor("wg", [128, 64], F32, kind="ExternalInput")
    w1_h = nc.dram_tensor("w1", [E, 128, 16384], F32, kind="ExternalInput")
    w3_h = nc.dram_tensor("w3", [E, 128, 16384], F32, kind="ExternalInput")
    w2_h = nc.dram_tensor("w2", [E, 128, 16384], F32, kind="ExternalInput")

    sb = nc.alloc_sbuf_tensor
    x_sb = sb("x_sb", [128, NT * 1024], F32)
    ln_sb = sb("ln_sb", [128, DCH * 2048], BF16)
    wgate = sb("wgate", [128, 64], BF16)
    coef = sb("coef", [128, NT * 8], F32)
    stats = sb("stats", [128, 32], F32)
    negm = sb("negm", [128, 8], F32)
    diag = sb("diag", [128, 5 * 128], BF16)
    identb = sb("identb", [128, 128], BF16)
    tril = sb("tril", [128, 128], F32)
    ln_tm = sb("ln_tm", [128, 1024], BF16)
    sg_sb = sb("sg_sb", [128, 2 * 512], BF16)
    gsc = sb("gsc", [128, 48], F32)

    AR = 102912
    arena = sb("arena", [128, AR], mybir.dt.uint8)
    base = nc.lookup_mloc(arena).addr
    at = nc.alloc_sbuf_tensor_at

    q_sb = at("q_sb", [128, 8 * 2048], BF16, offset=base + 0)
    k_sb = at("k_sb", [128, 2 * 2048], BF16, offset=base + 32768)
    v_sb = at("v_sb", [128, 16 * 256], BF16, offset=base + 40960)
    p_sb = at("p_sb", [128, 2048], BF16, offset=base + 49152)
    pt_sb = at("pt_sb", [128, 2048], BF16, offset=base + 53248)
    wq_sb = at("wq_sb", [128, 8192], BF16, offset=base + 57344)
    wo_sb = at("wo_sb", [128, 8192], BF16, offset=base + 73728)
    wk_sb = at("wk_sb", [128, 2048], BF16, offset=base + 90112)
    wv_sb = at("wv_sb", [128, 2048], BF16, offset=base + 94208)
    h_sb = at("h_sb", [128, 16384], BF16, offset=base + 0)
    w1_sb = at("w1_sb", [128, 8192], BF16, offset=base + 32768)
    w3_sb = at("w3_sb", [128, 8192], BF16, offset=base + 49152)
    w2_sb = at("w2_sb", [128, 16384], BF16, offset=base + 65536)
    dump = at("dump", [128, 1024], BF16, offset=base + 0)
    ln_tm2 = at("ln_tm2", [128, 1024], BF16, offset=base + 2048)
    p2_sb = at("p2_sb", [128, 2048], BF16, offset=base + 98816)
    p3_sb = sb("p3_sb", [128, 2048], BF16)
    p4_sb = at("p4_sb", [128, 2048], BF16, offset=base + 90112)
    p5_sb = at("p5_sb", [128, 2048], BF16, offset=base + 94208)

    psS = [nc.alloc_psum_tensor("psSa", [128, 512], F32),
           nc.alloc_psum_tensor("psSb", [128, 512], F32)]
    psPT = [nc.alloc_psum_tensor("psPTa", [128, 1024], F32),
            nc.alloc_psum_tensor("psPTb", [128, 1024], F32)]
    psO = [nc.alloc_psum_tensor("psOa", [128, 512], F32),
           nc.alloc_psum_tensor("psOb", [128, 512], F32)]

    P = Plan()

    P.emit("g", "memset", (identb.ap(),), {"constant": 0.0}, inc="gS")
    P.emit("g", None, waits=[("gS", P.n("gS"))])
    P.emit("g", "affine_select", (), {
        "out": identb.ap(), "in_": identb.ap(),
        "compare_op": OP.not_equal, "fill": 1.0,
        "base": 0, "pattern": [[-1, 128]], "channel_multiplier": 1})
    P.emit("g", "memset", (tril.ap(),), {"constant": 0.0}, inc="gS")
    P.emit("g", None, waits=[("gS", P.n("gS"))])
    P.emit("g", "affine_select", (), {
        "out": tril.ap(), "in_": tril.ap(),
        "compare_op": OP.is_ge, "fill": NEGBIG,
        "base": 0, "pattern": [[-1, 128]], "channel_multiplier": 1})
    P.emit("g", "memset", (diag.ap(),), {"constant": 0.0})
    P.emit("g", "dma_start", (wgate[:], wg_h.ap()), inc="wG", incby=16)
    P.emit("g", "nop", inc="gS")
    consts_g = P.n("gS")
    wg_done = P.n("wG")
    CW = [("gS", consts_g)]

    state = {
        "negm_guard": [0] * 8,
        "it": 0,
        "sq_guard": [0, 0],
        "ln_tm_guard": [0, 0],
        "lacc_guard": [0, 0],
        "diag_guard": [0] * 5,
    }

    def xcol(t):
        return 1024 * t

    def lncol(c, j=0):
        return 2048 * c + j

    def tT(out_ap, in_ap, waits=(), inc=None):
        return P.emit("t", "matmul", (out_ap, in_ap, identb[:]),
                      {"start": True, "stop": True}, waits=waits, inc=inc)

    def tmark():
        P.cnt["tC"] += 1
        meth, args, kw, waits, inc, incby = P.ops["t"][-2]
        assert meth == "matmul" and inc is None, (meth, inc)
        P.ops["t"][-2] = (meth, args, kw, waits, "tC", 1)
        return P.cnt["tC"]

    def emit_layernorm(dep_waits_fn):
        state["lnbank0"] = 0
        state["lnbank1"] = 0
        for t in range(NT):
            sl = 16 * (t % 2)
            lnb = ln_tm if t % 2 == 0 else ln_tm2

            def S(c):
                return stats[:, sl + c:sl + c + 1]

            xw = list(dep_waits_fn(t))
            xt = x_sb[:, xcol(t):xcol(t) + 1024]
            sq_w = xw + ([("vS", state["sq_guard"][t % 2])]
                         if state["sq_guard"][t % 2] else [])
            P.emit("s", "activation", (dump[:], xt, AF.Square),
                   {"accum_out": S(1)}, waits=sq_w, inc="sS")
            ss_done = P.n("sS")
            P.emit("v", "tensor_reduce", (S(0), xt, AX, OP.add),
                   {"negate": True}, waits=xw, inc="vS")
            P.emit("v", "tensor_scalar", (S(2), S(0), 1.0 / D, None, OP.mult))
            P.emit("v", "tensor_scalar", (S(3), S(1), 1.0 / D, None, OP.mult),
                   waits=[("sS", ss_done)], inc="vS")
            state["sq_guard"][t % 2] = P.n("vS")
            P.emit("v", "tensor_tensor", (S(4), S(2), S(2), OP.mult))
            P.emit("v", "tensor_tensor", (S(5), S(3), S(4), OP.subtract))
            P.emit("v", "tensor_scalar", (S(6), S(5), EPS, None, OP.add))
            P.emit("v", "reciprocal", (S(7), S(6)), inc="vS")
            rv_done = P.n("vS")
            P.emit("s", "activation", (S(8), S(7), AF.Sqrt),
                   waits=[("vS", rv_done)], inc="sS")
            rs_done = P.n("sS")
            lw = [("sS", rs_done)]
            if state["ln_tm_guard"][t % 2]:
                lw.append(("tC", state["ln_tm_guard"][t % 2]))
            P.emit("v", "tensor_scalar", (lnb[:], xt, S(2), S(8),
                                          OP.add, OP.mult),
                   waits=lw, inc="vS")
            lnt_done = P.n("vS")
            for b4 in range(4):
                seti = b4 % 2
                bank = psPT[seti]
                bw = CW + [("vS", lnt_done)]
                g = state[f"lnbank{seti}"]
                if g:
                    bw.append(g)
                for j in range(2):
                    c = 2 * b4 + j
                    tT(bank[:, 512 * j:512 * j + 128],
                       lnb[:, 128 * c:128 * c + 128],
                       waits=bw if j == 0 else ())
                tr_done = tmark()
                src = bass.AP(bank, 0, [[1024, 128], [512, 2], [1, 128]])
                out_ap = bass.AP(ln_sb, 2048 * 2 * b4 + 128 * t,
                                 [[DCH * 2048, 128], [2048, 2], [1, 128]])
                eng = "v" if b4 < 2 else "s"
                if eng == "v":
                    P.emit("v", "tensor_copy", (out_ap, src),
                           waits=[("tC", tr_done)], inc="vS")
                    state[f"lnbank{seti}"] = ("vS", P.n("vS"))
                else:
                    P.emit("s", "copy", (out_ap, src),
                           waits=[("tC", tr_done)], inc="sS")
                    state[f"lnbank{seti}"] = ("sS", P.n("sS"))
                state["ln_tm_guard"][t % 2] = tr_done
        state["ln_done_s"] = P.n("sS")
        return P.n("vS")

    def emit_xload(seq):
        xv = x_hbm.ap()[seq * T:(seq + 1) * T, :].rearrange("(t p) c -> p t c", p=128)
        xs = x_sb[:].rearrange("p (t c) -> p t c", c=1024)
        for g2 in range(2):
            P.emit("y", "dma_start", (xs[:, 8 * g2:8 * g2 + 8, :], xv[:, 8 * g2:8 * g2 + 8, :]),
                   inc="dX", incby=16)
        state["xload"] = P.n("dX")

    def emit_store(seq, half):
        yv = y_hbm.ap()[seq * T + 1024 * half:seq * T + 1024 * half + 1024, :]
        yv = yv.rearrange("(t p) c -> p t c", p=128)
        xs = x_sb[:].rearrange("p (t c) -> p t c", c=1024)
        P.emit("y", "dma_start", (yv, xs[:, 8 * half:8 * half + 8, :]),
               waits=[("vS", P.n("vS"))], inc="dO", incby=16)

    def emit_attention(seq, level=4):
        xload = state["xload"]

        if "attw" not in DEBUG_SKIP:
            P.emit("g", "dma_start", (wq_sb[:], wq_h.ap()), waits=CW, inc="wA", incby=16)
            P.emit("g", "dma_start", (wk_sb[:], wk_h.ap()), inc="wA", incby=16)
            P.emit("g", "dma_start", (wv_sb[:], wv_h.ap()), inc="wA", incby=16)
        wqkv_done = ("wA", P.n("wA"))
        if "attw" not in DEBUG_SKIP:
            P.emit("g", "dma_start", (wo_sb[:], wo_h.ap()), inc="wB", incby=16)
        wo_done = ("wB", P.n("wB"))

        xw = [("dX", xload)]
        ln_done = emit_layernorm(lambda t: xw)
        if level < 2:
            return

        mmw = [("vS", ln_done), ("sS", state["ln_done_s"]), wqkv_done]
        ev_guard = [0, 0]
        for qc in range(0 if "qproj" not in DEBUG_SKIP else 0, DCH if "qproj" not in DEBUG_SKIP else 0):
            for ng in range(4):
                bi = (qc * 4 + ng) % 2
                bank = psS[bi]
                bw = list(mmw)
                if ev_guard[bi]:
                    bw.append(("sS", ev_guard[bi]))
                for jd in range(DCH):
                    P.emit("t", "matmul",
                           (bank[:],
                            wq_sb[:, 1024 * qc + 128 * jd:1024 * qc + 128 * jd + 128],
                            ln_sb[:, lncol(jd, 512 * ng):lncol(jd, 512 * ng) + 512]),
                           {"start": jd == 0, "stop": jd == DCH - 1},
                           waits=bw if jd == 0 else ())
                mm_done = tmark()
                P.emit("s", "copy",
                       (q_sb[:, 2048 * qc + 512 * ng:2048 * qc + 512 * ng + 512], bank[:]),
                       waits=[("tC", mm_done)], inc="sS")
                ev_guard[bi] = P.n("sS")
        for kc in range(2 if "kproj" not in DEBUG_SKIP else 0):
            for ng in range(4):
                bi = (kc * 4 + ng) % 2
                bank = psS[bi]
                bw = [("sS", ev_guard[bi])]
                for jd in range(DCH):
                    P.emit("t", "matmul",
                           (bank[:],
                            wk_sb[:, 1024 * kc + 128 * jd:1024 * kc + 128 * jd + 128],
                            ln_sb[:, lncol(jd, 512 * ng):lncol(jd, 512 * ng) + 512]),
                           {"start": jd == 0, "stop": jd == DCH - 1},
                           waits=bw if jd == 0 else ())
                mm_done = tmark()
                P.emit("s", "copy",
                       (k_sb[:, 2048 * kc + 512 * ng:2048 * kc + 512 * ng + 512], bank[:]),
                       waits=[("tC", mm_done)], inc="sS")
                ev_guard[bi] = P.n("sS")
        v_guard = [0, 0]
        for vt in range(NT if "vproj" not in DEBUG_SKIP else 0):
            bank = psO[vt % 2]
            bw = [("vS", v_guard[vt % 2])] if v_guard[vt % 2] else list(mmw)
            for jd in range(DCH):
                P.emit("t", "matmul",
                       (bank[:, 0:256],
                        ln_sb[:, lncol(jd, 128 * vt):lncol(jd, 128 * vt) + 128],
                        wv_sb[:, 256 * jd:256 * jd + 256]),
                       {"start": jd == 0, "stop": jd == DCH - 1},
                       waits=bw if jd == 0 else ())
            mm_done = tmark()
            P.emit("v", "tensor_copy", (v_sb[:, 256 * vt:256 * vt + 256], bank[:, 0:256]),
                   waits=[("tC", mm_done)], inc="vS")
            v_guard[vt % 2] = P.n("vS")
        v_all = P.n("vS")
        if level < 3:
            return

        sbank_guard = [P.n("sS"), P.n("sS")]
        ptbank_guard = [0, 0]
        po_guard = [v_all, v_all]

        def stageA(c):
            it, qi = c["it"], c["qi"]
            nk = c["nk"]
            ngroups = (nk + 511) // 512
            c["ngroups"] = ngroups
            glast = ngroups - 1
            order = [glast] + list(range(glast))
            negs = it % 8
            pb = [p_sb, p2_sb, p3_sb, p4_sb, p5_sb][it % 5]
            c["pb"] = pb
            lset = it % 2
            lbase = 10 if lset == 0 else 26
            lsumc = 14 if lset == 0 else 30
            linvc = 15 if lset == 0 else 31
            m_done = 0
            for gi, g in enumerate(order):
                n = min(512, nk - 512 * g)
                bi = gi % 2
                bank = psS[bi]
                P.emit("t", "matmul",
                       (bank[:, 0:n],
                        q_sb[c["po_h"]:c["po_h"] + 64,
                             2048 * c["qc_h"] + 128 * qi:2048 * c["qc_h"] + 128 * qi + 128],
                        k_sb[c["ko_h"]:c["ko_h"] + 64,
                             2048 * c["kc_h"] + 512 * g:2048 * c["kc_h"] + 512 * g + n]),
                       {"start": True, "stop": True},
                       waits=[("sS", sbank_guard[bi])])
                s_done = tmark()
                if gi == 0:
                    doff = (nk - 128) - 512 * g
                    mw = [("tC", s_done)]
                    if state["negm_guard"][negs]:
                        mw.append(("sS", state["negm_guard"][negs]))
                    P.emit("v", "tensor_tensor",
                           (bank[:, doff:doff + 128], bank[:, doff:doff + 128],
                            tril[:], OP.add), waits=mw)
                    P.emit("v", "tensor_reduce",
                           (negm[:, negs:negs + 1], bank[:, doff:doff + 128], AX, OP.max),
                           {"negate": True}, inc="vS")
                    m_done = P.n("vS")
                ew = [("tC", s_done), ("vS", m_done)]
                if gi == 0 and state["lacc_guard"][lset]:
                    ew.append(("vS", state["lacc_guard"][lset]))
                P.emit("s", "activation",
                       (pb[:, 512 * g:512 * g + n], bank[:, 0:n], AF.Exp),
                       {"bias": negm[:, negs:negs + 1], "scale": 0.125,
                        "accum_out": stats[:, lbase + gi:lbase + gi + 1]},
                       waits=ew, inc="sS")
                sbank_guard[bi] = P.n("sS")
            state["negm_guard"][negs] = P.n("sS")
            exp_all = P.n("sS")
            P.emit("v", "tensor_reduce",
                   (stats[:, lsumc:lsumc + 1], stats[:, lbase:lbase + ngroups], AX, OP.add),
                   waits=[("sS", exp_all)], inc="vS")
            state["lacc_guard"][lset] = P.n("vS")
            P.emit("v", "reciprocal", (stats[:, linvc:linvc + 1], stats[:, lsumc:lsumc + 1]))
            dslot = it % 5
            dw = [("gS", consts_g)]
            if state["diag_guard"][dslot]:
                dw.append(("tC", state["diag_guard"][dslot]))
            P.emit("v", "tensor_scalar",
                   (diag[:, 128 * dslot:128 * dslot + 128], identb[:],
                    stats[:, linvc:linvc + 1], None, OP.mult),
                   waits=dw, inc="vS")
            c["dslot"] = dslot
            c["scale_done"] = P.n("vS")

        def stageB(c):
            if c is None:
                return
            it, qi, kv = c["it"], c["qi"], c["kv"]
            pb = c["pb"]
            nbat = (qi + 1 + 1) // 2
            evac_v = []
            for b in range(nbat):
                k0, k1 = 2 * b, min(2 * b + 2, qi + 1)
                seti = b % 2
                bank = psPT[seti]
                bw = [("vS", c["scale_done"])]
                if ptbank_guard[seti]:
                    bw.append(ptbank_guard[seti])
                for kt in range(k0, k1):
                    P.emit("t", "matmul",
                           (bank[:, 512 * (kt - k0):512 * (kt - k0) + 128],
                            pb[:, 128 * kt:128 * kt + 128],
                            diag[:, 128 * c["dslot"]:128 * c["dslot"] + 128]),
                           {"start": True, "stop": True},
                           waits=bw if kt == k0 else ())
                tr_done = tmark()
                state["diag_guard"][c["dslot"]] = tr_done
                src = bass.AP(bank, 0, [[1024, 128], [512, k1 - k0], [1, 128]])
                if b % 2 == 0:
                    P.emit("v", "tensor_copy",
                           (pt_sb[:, 128 * k0:128 * k1], src),
                           waits=[("tC", tr_done)], inc="vS")
                    ptbank_guard[seti] = ("vS", P.n("vS"))
                    evac_v.append((k0, k1, ("vS", P.n("vS"))))
                else:
                    P.emit("s", "copy",
                           (pt_sb[:, 128 * k0:128 * k1], src),
                           waits=[("tC", tr_done)], inc="sS")
                    ptbank_guard[seti] = ("sS", P.n("sS"))
                    evac_v.append((k0, k1, ("sS", P.n("sS"))))
            po_i = it % 2
            for (k0, k1, ev) in evac_v:
                for kt in range(k0, k1):
                    bw = [ev]
                    if kt == 0:
                        bw.append(("vS", po_guard[po_i]))
                    P.emit("t", "matmul",
                           (psO[po_i][0:64, 0:128],
                            v_sb[:, 256 * kt + 64 * kv:256 * kt + 64 * kv + 64],
                            pt_sb[:, 128 * kt:128 * kt + 128]),
                           {"start": kt == 0, "stop": kt == qi},
                           waits=bw)
            av_done = tmark()
            P.emit("v", "tensor_copy",
                   (q_sb[c["po_h"]:c["po_h"] + 64,
                         2048 * c["qc_h"] + 128 * qi:2048 * c["qc_h"] + 128 * qi + 128],
                    psO[po_i][0:64, 0:128]),
                   waits=[("tC", av_done)], inc="vS")
            po_guard[po_i] = P.n("vS")

        from collections import deque
        pend = deque()
        DEPTH = 4
        for h in range(HQ if DEBUG_HQ is None else DEBUG_HQ):
            kv = h // 4
            if h in HEADS_A:
                qc_h, po_h = HEADS_A.index(h), 0
            else:
                qc_h, po_h = HEADS_B.index(h), 64
            kc_h, ko_h = kv // 2, 64 * (kv % 2)
            for qi in range(NT if DEBUG_NQI is None else DEBUG_NQI):
                it = state["it"]
                state["it"] += 1
                c = {"it": it, "qi": qi, "kv": kv, "qc_h": qc_h, "po_h": po_h,
                     "kc_h": kc_h, "ko_h": ko_h, "nk": 128 * (qi + 1)}
                stageA(c)
                pend.append(c)
                if len(pend) > DEPTH:
                    stageB(pend.popleft())
        while pend:
            stageB(pend.popleft())

        ctx_done = P.n("vS")
        if level < 4:
            return

        out_guard = [None, None, None, None]
        for qi in range(NT):
            pair = psS if qi % 2 == 0 else psO
            for nh in range(2):
                bank = pair[nh]
                gidx = (0 if qi % 2 == 0 else 2) + nh
                bw = [("vS", ctx_done), wo_done]
                if qi % 2 == 0:
                    bw.append(("sS", sbank_guard[nh]))
                if out_guard[gidx]:
                    bw.append(out_guard[gidx])
                for cc in range(DCH):
                    P.emit("t", "matmul",
                           (bank[:],
                            q_sb[:, 2048 * cc + 128 * qi:2048 * cc + 128 * qi + 128],
                            wo_sb[:, 1024 * cc + 512 * nh:1024 * cc + 512 * nh + 512]),
                           {"start": cc == 0, "stop": cc == DCH - 1},
                           waits=bw if cc == 0 else ())
                mm_done = tmark()
                P.emit("v", "tensor_add",
                       (x_sb[:, xcol(qi) + 512 * nh:xcol(qi) + 512 * nh + 512],
                        x_sb[:, xcol(qi) + 512 * nh:xcol(qi) + 512 * nh + 512],
                        bank[:]),
                       waits=[("tC", mm_done)], inc="vS")
                out_guard[gidx] = ("vS", P.n("vS"))

    def emit_ffn(seq):
        xw = [("dX", state["xload"])]
        ln_done = emit_layernorm(lambda t: xw)
        g_guard = [0, 0]
        for t in range(NT):
            bank = psS[t % 2]
            bw = [("vS", ln_done), ("sS", state["ln_done_s"]), ("wG", wg_done)]
            if g_guard[t % 2]:
                bw.append(("vS", g_guard[t % 2]))
            for jd in range(DCH):
                P.emit("t", "matmul",
                       (bank[:, 0:8],
                        ln_sb[:, lncol(jd, 128 * t):lncol(jd, 128 * t) + 128],
                        wgate[:, 8 * jd:8 * jd + 8]),
                       {"start": jd == 0, "stop": jd == DCH - 1},
                       waits=bw if jd == 0 else ())
            mm_done = tmark()
            P.emit("v", "tensor_copy", (coef[:, 8 * t:8 * t + 8], bank[:, 0:8]),
                   waits=[("tC", mm_done)], inc="vS")
            g_guard[t % 2] = P.n("vS")
        logits_done = P.n("vS")
        coefd = []
        for t in range(NT):
            lg = coef[:, 8 * t:8 * t + 8]
            P.emit("v", "tensor_reduce", (gsc[:, 8:9], lg, AX, OP.max),
                   {"negate": True})
            P.emit("v", "tensor_scalar", (gsc[:, 16:24], lg, gsc[:, 8:9], None, OP.add))
            P.emit("v", "tensor_scalar", (gsc[:, 24:32], gsc[:, 16:24], 0.0, None, OP.is_equal))
            P.emit("v", "scalar_tensor_tensor",
                   (gsc[:, 32:40], gsc[:, 24:32], NEGBIG, lg, OP.mult, OP.add))
            P.emit("v", "tensor_reduce", (gsc[:, 9:10], gsc[:, 32:40], AX, OP.max),
                   {"negate": True})
            P.emit("v", "tensor_tensor", (gsc[:, 10:11], gsc[:, 9:10], gsc[:, 8:9], OP.subtract),
                   inc="vS")
            pre_s = P.n("vS")
            P.emit("s", "activation", (gsc[:, 11:12], gsc[:, 10:11], AF.Exp),
                   {"scale": -1.0}, waits=[("vS", pre_s)], inc="sS")
            e_done = P.n("sS")
            P.emit("v", "tensor_scalar", (gsc[:, 12:13], gsc[:, 11:12], 1.0, None, OP.add),
                   waits=[("sS", e_done)])
            P.emit("v", "reciprocal", (gsc[:, 13:14], gsc[:, 12:13]), inc="vS")
            rd_done = P.n("vS")
            P.emit("s", "activation", (gsc[:, 40:48], lg, AF.Exp),
                   {"bias": gsc[:, 8:9]}, inc="sS")
            e1_done = P.n("sS")
            P.emit("v", "tensor_scalar", (gsc[:, 16:24], lg, gsc[:, 9:10], None, OP.add))
            P.emit("v", "tensor_scalar", (gsc[:, 24:32], gsc[:, 16:24], 0.0, None, OP.is_ge))
            P.emit("v", "scalar_tensor_tensor",
                   (coef[:, 8 * t:8 * t + 8], gsc[:, 40:48], gsc[:, 13:14], gsc[:, 24:32],
                    OP.mult, OP.mult),
                   waits=[("sS", e1_done), ("vS", rd_done)], inc="vS")
            coefd.append(P.n("vS"))
        gate_done = logits_done

        gu_guard = [0, 0, 0, 0]
        sg_guard = [0, 0]
        y_guard = [0, 0]
        w1_guard = [0, 0]
        w3_guard = [0, 0]
        w2_guard = 0
        gu_sets = [psS[0], psS[1], psO[0], psO[1]]
        for half in range(2):
            for e in range(E):
                w_done = {}

                def emit_w13(grp):
                    sh = grp % 2
                    s1 = "u1a" if sh == 0 else "u1b"
                    s3 = "u3a" if sh == 0 else "u3b"
                    bw = [("tC", w1_guard[sh])] if w1_guard[sh] else []
                    P.emit("g", "dma_start",
                           (w1_sb[:, 4096 * sh:4096 * sh + 4096],
                            w1_h.ap()[e, :, 4096 * grp:4096 * grp + 4096]),
                           waits=bw, inc=s1, incby=16)
                    w_done[("w1", grp)] = (s1, P.n(s1))
                    bw = [("tC", w3_guard[sh])] if w3_guard[sh] else []
                    P.emit("g", "dma_start",
                           (w3_sb[:, 4096 * sh:4096 * sh + 4096],
                            w3_h.ap()[e, :, 4096 * grp:4096 * grp + 4096]),
                           waits=bw, inc=s3, incby=16)
                    w_done[("w3", grp)] = (s3, P.n(s3))

                emit_w13(0)
                emit_w13(1)
                w2w = [("tC", w2_guard)] if w2_guard else []
                for grp in range(4):
                    P.emit("g", "dma_start",
                           (w2_sb[:, 4096 * grp:4096 * grp + 4096],
                            w2_h.ap()[e, :, 4096 * grp:4096 * grp + 4096]),
                           waits=w2w if grp == 0 else (), inc="u2", incby=16)
                w2_done = ("u2", P.n("u2"))

                gu_idx = 0
                last_slot_use = {0: 0, 1: 0}
                for hc in range(HCH):
                    sh = (hc // 4) % 2
                    scol = 4096 * sh + 1024 * (hc % 4)
                    grp = hc // 4
                    for tg in range(2):
                        tok0 = 1024 * half + 512 * tg
                        pg = gu_sets[gu_idx % 4]
                        pu = gu_sets[(gu_idx + 1) % 4]
                        bw_g = [w_done[("w1", grp)], ("vS", gate_done)]
                        if gu_guard[gu_idx % 4]:
                            bw_g.append(("vS", gu_guard[gu_idx % 4]))
                        for jd in range(DCH):
                            P.emit("t", "matmul",
                                   (pg[:],
                                    w1_sb[:, scol + 128 * jd:scol + 128 * jd + 128],
                                    ln_sb[:, lncol(jd, tok0):lncol(jd, tok0) + 512]),
                                   {"start": jd == 0, "stop": jd == DCH - 1},
                                   waits=bw_g if jd == 0 else ())
                        g_done = tmark()
                        bw_u = [w_done[("w3", grp)]]
                        if gu_guard[(gu_idx + 1) % 4]:
                            bw_u.append(("vS", gu_guard[(gu_idx + 1) % 4]))
                        for jd in range(DCH):
                            P.emit("t", "matmul",
                                   (pu[:],
                                    w3_sb[:, scol + 128 * jd:scol + 128 * jd + 128],
                                    ln_sb[:, lncol(jd, tok0):lncol(jd, tok0) + 512]),
                                   {"start": jd == 0, "stop": jd == DCH - 1},
                                   waits=bw_u if jd == 0 else ())
                        u_done = tmark()
                        last_slot_use[sh] = u_done
                        ss = gu_idx % 2
                        sw = [("tC", g_done)]
                        if sg_guard[ss]:
                            sw.append(("vS", sg_guard[ss]))
                        P.emit("s", "activation",
                               (sg_sb[:, 512 * ss:512 * ss + 512], pg[:], AF.Silu),
                               waits=sw, inc="sS")
                        silu_done = P.n("sS")
                        hcol = 1024 * hc + 512 * tg
                        P.emit("v", "tensor_tensor",
                               (h_sb[:, hcol:hcol + 512], sg_sb[:, 512 * ss:512 * ss + 512],
                                pu[:], OP.mult),
                               waits=[("sS", silu_done), ("tC", u_done)], inc="vS")
                        gu_guard[gu_idx % 4] = P.n("vS")
                        gu_guard[(gu_idx + 1) % 4] = P.n("vS")
                        sg_guard[ss] = P.n("vS")
                        gu_idx += 2
                    if hc % 4 == 3:
                        w1_guard[sh] = last_slot_use[sh]
                        w3_guard[sh] = last_slot_use[sh]
                        nxt = hc // 4 + 2
                        if nxt < 4:
                            emit_w13(nxt)
                h_done = P.n("vS")

                for tg in range(2):
                    for j2 in range(4):
                        jh = 4 * tg + j2
                        tseq = 8 * half + jh
                        yset = j2 % 2
                        ybank = psPT[yset]
                        bw = [("vS", h_done), w2_done]
                        if y_guard[yset]:
                            bw.append(("vS", y_guard[yset]))
                        for hc in range(HCH):
                            for nh in range(2):
                                P.emit("t", "matmul",
                                       (ybank[:, 512 * nh:512 * nh + 512],
                                        h_sb[:, 1024 * hc + 128 * jh:1024 * hc + 128 * jh + 128],
                                        w2_sb[:, 1024 * hc + 512 * nh:1024 * hc + 512 * nh + 512]),
                                       {"start": hc == 0, "stop": hc == HCH - 1},
                                       waits=bw if (hc == 0 and nh == 0) else ())
                        y_done = tmark()
                        for nh in range(2):
                            xa = x_sb[:, xcol(tseq) + 512 * nh:xcol(tseq) + 512 * nh + 512]
                            P.emit("v", "scalar_tensor_tensor",
                                   (xa, ybank[:, 512 * nh:512 * nh + 512],
                                    coef[:, 8 * tseq + e:8 * tseq + e + 1],
                                    xa, OP.mult, OP.add),
                                   waits=([("tC", y_done), ("vS", coefd[tseq])]
                                          if nh == 0 else ()),
                                   inc="vS" if nh == 1 else None)
                        y_guard[yset] = P.n("vS")
                w2_guard = P.n("tC")
            emit_store(seq, half)

    attn_level = 0
    for ph in phases:
        if ph == "attn":
            attn_level = 4
        elif ph.startswith("attn"):
            attn_level = int(ph[4:])
    for seq in range(SEQS):
        emit_xload(seq)
        if attn_level:
            emit_attention(seq, attn_level)
            P.barrier()
        if "ffn" in phases:
            emit_ffn(seq)
            P.barrier()
        else:
            P.emit("v", "nop", inc="vS")
            for half in range(2):
                emit_store(seq, half)
            P.barrier()
    P.emit("y", None, waits=[("dO", P.n("dO"))])

    semh = {s: nc.alloc_semaphore(s) for s in SEMS}
    with nc.Block() as block:

        def runner(ops):
            def run(hdl):
                for (method, args, kwargs, waits, inc, incby) in ops:
                    for snm, val in waits:
                        hdl.wait_ge(semh[snm], val)
                    if method is None:
                        continue
                    ins = getattr(hdl, method)(*args, **kwargs)
                    if inc is not None:
                        ins.then_inc(semh[inc], incby)
            return run

        block.tensor(runner(P.ops["t"]))
        block.vector(runner(P.ops["v"]))
        block.scalar(runner(P.ops["s"]))
        block.gpsimd(runner(P.ops["g"]))
        block.sync(runner(P.ops["y"]))

    return nc


_CACHE = {}


def kernel(L0_attn_x, L0_attn_Wq, L0_attn_Wk, L0_attn_Wv, L0_attn_Wo,
           L0_ffn_w_gate, L0_ffn_w1, L0_ffn_w3, L0_ffn_w2):
    x = np.ascontiguousarray(np.asarray(L0_attn_x, dtype=np.float32))
    Wq = np.asarray(L0_attn_Wq, dtype=np.float32)
    Wk = np.asarray(L0_attn_Wk, dtype=np.float32)
    Wv = np.asarray(L0_attn_Wv, dtype=np.float32)
    Wo = np.asarray(L0_attn_Wo, dtype=np.float32)
    wg = np.asarray(L0_ffn_w_gate, dtype=np.float32)
    w1 = np.asarray(L0_ffn_w1, dtype=np.float32)
    w3 = np.asarray(L0_ffn_w3, dtype=np.float32)
    w2 = np.asarray(L0_ffn_w2, dtype=np.float32)

    colperm = np.concatenate([np.arange(64 * hh, 64 * hh + 64) for hh in HEAD_PERM])
    Wq = Wq[:, colperm]
    Wo = Wo[colperm, :]
    WqH = np.ascontiguousarray(
        Wq.reshape(8, 128, 8, 128).transpose(1, 2, 0, 3).reshape(128, 8192))
    WkH = np.ascontiguousarray(
        Wk.reshape(8, 128, 2, 128).transpose(1, 2, 0, 3).reshape(128, 2048))
    WvH = np.ascontiguousarray(
        Wv.reshape(8, 128, 256).transpose(1, 0, 2).reshape(128, 2048))
    WoH = np.ascontiguousarray(
        Wo.reshape(8, 128, 1024).transpose(1, 0, 2).reshape(128, 8192))
    WgH = np.ascontiguousarray(
        wg.reshape(8, 128, 8).transpose(1, 0, 2).reshape(128, 64))
    W1H = np.ascontiguousarray(
        w1.reshape(E, 8, 128, 16, 128).transpose(0, 2, 3, 1, 4).reshape(E, 128, 16384))
    W3H = np.ascontiguousarray(
        w3.reshape(E, 8, 128, 16, 128).transpose(0, 2, 3, 1, 4).reshape(E, 128, 16384))
    W2H = np.ascontiguousarray(
        w2.reshape(E, 16, 128, 1024).transpose(0, 2, 1, 3).reshape(E, 128, 16384))

    key = tuple(PHASES)
    if key not in _CACHE:
        _CACHE[key] = build_program()
    nc = _CACHE[key]

    in_maps = []
    for c in range(NCORES):
        xc = np.ascontiguousarray(x[2 * c:2 * c + 2].reshape(SEQS * T, D))
        in_maps.append({
            "xin": xc, "wq": WqH, "wk": WkH, "wv": WvH, "wo": WoH,
            "wg": WgH, "w1": W1H, "w3": W3H, "w2": W2H,
        })
    res = run_bass_kernel_spmd(nc, in_maps, core_ids=list(range(NCORES)))
    out = np.concatenate(
        [res.results[c]["yout"].reshape(SEQS, T, D) for c in range(NCORES)], axis=0)
    return out


# version 10
# speedup vs baseline: 1.2063x; 1.2063x over previous
import numpy as np
import concourse.bass as bass
import concourse.mybir as mybir
from concourse.bass_utils import run_bass_kernel_spmd

F32 = mybir.dt.float32
BF16 = mybir.dt.bfloat16
AX = mybir.AxisListType.X
OP = mybir.AluOpType
AF = mybir.ActivationFunctionType

NCORES = 8
SEQS = 2
T = 2048
D = 1024
HQ, HKV, HD = 16, 4, 64
E, H = 8, 2048
NT = T // 128
DCH = D // 128
HCH = H // 128
EPS = 1e-5
NEGBIG = -1.0e30

ENGS = ("t", "v", "s", "g", "y")
SEMS = ("dX", "dO", "tS", "tC", "vS", "sS", "gS",
        "wA", "wB", "wG", "u1a", "u1b", "u3a", "u3b", "u2")

HEADS_A = [0, 1, 2, 3, 8, 9, 10, 11]
HEADS_B = [4, 5, 6, 7, 12, 13, 14, 15]
HEAD_PERM = []
for _c in range(8):
    HEAD_PERM.append(HEADS_A[_c])
    HEAD_PERM.append(HEADS_B[_c])

PHASES = ("attn", "ffn")
DEBUG_SKIP = set()
DEBUG_HQ = None
DEBUG_NQI = None


MAX_MM_INFLIGHT = 3


class Plan:
    def __init__(self):
        self.ops = {e: [] for e in ENGS}
        self.cnt = {s: 0 for s in SEMS}
        self.lastwait = {}

    def n(self, sem):
        return self.cnt[sem]

    def emit(self, eng, method, args=(), kwargs=None, waits=(), inc=None, incby=1):
        if eng == "t" and method == "matmul":
            waits = tuple(waits) + (("tS", self.cnt["tS"] + 1 - MAX_MM_INFLIGHT),)
            real = []
            for snm, val in waits:
                if val > 0 and self.lastwait.get((eng, snm), -1) < val:
                    real.append((snm, val))
                    self.lastwait[(eng, snm)] = val
            self.ops[eng].append((method, args, kwargs or {}, tuple(real), None, 0))
            self.cnt["tS"] += 1
            self.ops[eng].append(("nop", (), {}, (), "tS", 1))
            return self.cnt["tS"]
        if eng == "v" and method is not None:
            waits = tuple(waits) + (("vS", self.cnt["vS"]),)
            inc, incby = "vS", 1
        elif eng == "s" and method is not None:
            waits = tuple(waits) + (("sS", self.cnt["sS"]),)
            inc, incby = "sS", 1
        real = []
        for snm, val in waits:
            if val > 0 and self.lastwait.get((eng, snm), -1) < val:
                real.append((snm, val))
                self.lastwait[(eng, snm)] = val
        if inc is not None:
            self.cnt[inc] += incby
        self.ops[eng].append((method, args, kwargs or {}, tuple(real), inc, incby))
        return self.cnt[inc] if inc is not None else 0

    def barrier(self):
        self.emit("t", "nop", inc="tS")
        self.emit("v", "nop", inc="vS")
        self.emit("s", "nop", inc="sS")
        self.emit("g", "nop", inc="gS")
        snap = [(s, self.cnt[s]) for s in SEMS]
        for e in ENGS:
            self.emit(e, None, waits=snap)


def build_program(phases=None):
    phases = PHASES if phases is None else phases
    nc = bass.Bass("TRN2", target_bir_lowering=False)

    x_hbm = nc.dram_tensor("xin", [SEQS * T, D], F32, kind="ExternalInput")
    y_hbm = nc.dram_tensor("yout", [SEQS * T, D], F32, kind="ExternalOutput")
    wq_h = nc.dram_tensor("wq", [128, 8192], F32, kind="ExternalInput")
    wk_h = nc.dram_tensor("wk", [128, 2048], F32, kind="ExternalInput")
    wv_h = nc.dram_tensor("wv", [128, 2048], F32, kind="ExternalInput")
    wo_h = nc.dram_tensor("wo", [128, 8192], F32, kind="ExternalInput")
    wg_h = nc.dram_tensor("wg", [128, 64], F32, kind="ExternalInput")
    w1_h = nc.dram_tensor("w1", [E, 128, 16384], F32, kind="ExternalInput")
    w3_h = nc.dram_tensor("w3", [E, 128, 16384], F32, kind="ExternalInput")
    w2_h = nc.dram_tensor("w2", [E, 128, 16384], F32, kind="ExternalInput")

    sb = nc.alloc_sbuf_tensor
    x_sb = sb("x_sb", [128, NT * 1024], F32)
    ln_sb = sb("ln_sb", [128, DCH * 2048], BF16)
    wgate = sb("wgate", [128, 64], BF16)
    coef = sb("coef", [128, NT * 8], F32)
    stats = sb("stats", [128, 32], F32)
    negm = sb("negm", [128, 8], F32)
    diag = sb("diag", [128, 5 * 128], BF16)
    identb = sb("identb", [128, 128], BF16)
    tril = sb("tril", [128, 128], F32)
    ln_tm = sb("ln_tm", [128, 1024], BF16)
    sg_sb = sb("sg_sb", [128, 2 * 512], BF16)
    gsc = sb("gsc", [128, 48], F32)

    AR = 102912
    arena = sb("arena", [128, AR], mybir.dt.uint8)
    base = nc.lookup_mloc(arena).addr
    at = nc.alloc_sbuf_tensor_at

    q_sb = at("q_sb", [128, 8 * 2048], BF16, offset=base + 0)
    k_sb = at("k_sb", [128, 2 * 2048], BF16, offset=base + 32768)
    v_sb = at("v_sb", [128, 16 * 256], BF16, offset=base + 40960)
    p_sb = at("p_sb", [128, 2048], BF16, offset=base + 49152)
    pt_sb = at("pt_sb", [128, 2048], BF16, offset=base + 53248)
    wq_sb = at("wq_sb", [128, 8192], BF16, offset=base + 57344)
    wo_sb = at("wo_sb", [128, 8192], BF16, offset=base + 73728)
    wk_sb = at("wk_sb", [128, 2048], BF16, offset=base + 90112)
    wv_sb = at("wv_sb", [128, 2048], BF16, offset=base + 94208)
    h_sb = at("h_sb", [128, 16384], BF16, offset=base + 0)
    w1_sb = at("w1_sb", [128, 8192], BF16, offset=base + 32768)
    w3_sb = at("w3_sb", [128, 8192], BF16, offset=base + 49152)
    w2_sb = at("w2_sb", [128, 16384], BF16, offset=base + 65536)
    dump = at("dump", [128, 1024], BF16, offset=base + 0)
    ln_tm2 = at("ln_tm2", [128, 1024], BF16, offset=base + 2048)
    p2_sb = at("p2_sb", [128, 2048], BF16, offset=base + 98816)
    p3_sb = sb("p3_sb", [128, 2048], BF16)
    p4_sb = at("p4_sb", [128, 2048], BF16, offset=base + 90112)
    p5_sb = at("p5_sb", [128, 2048], BF16, offset=base + 94208)

    psS = [nc.alloc_psum_tensor("psSa", [128, 512], F32),
           nc.alloc_psum_tensor("psSb", [128, 512], F32)]
    psPT = [nc.alloc_psum_tensor("psPTa", [128, 1024], F32),
            nc.alloc_psum_tensor("psPTb", [128, 1024], F32)]
    psO = [nc.alloc_psum_tensor("psOa", [128, 512], F32),
           nc.alloc_psum_tensor("psOb", [128, 512], F32)]

    P = Plan()

    P.emit("g", "memset", (identb.ap(),), {"constant": 0.0}, inc="gS")
    P.emit("g", None, waits=[("gS", P.n("gS"))])
    P.emit("g", "affine_select", (), {
        "out": identb.ap(), "in_": identb.ap(),
        "compare_op": OP.not_equal, "fill": 1.0,
        "base": 0, "pattern": [[-1, 128]], "channel_multiplier": 1})
    P.emit("g", "memset", (tril.ap(),), {"constant": 0.0}, inc="gS")
    P.emit("g", None, waits=[("gS", P.n("gS"))])
    P.emit("g", "affine_select", (), {
        "out": tril.ap(), "in_": tril.ap(),
        "compare_op": OP.is_ge, "fill": NEGBIG,
        "base": 0, "pattern": [[-1, 128]], "channel_multiplier": 1})
    P.emit("g", "memset", (diag.ap(),), {"constant": 0.0})
    P.emit("g", "dma_start", (wgate[:], wg_h.ap()), inc="wG", incby=16)
    P.emit("g", "nop", inc="gS")
    consts_g = P.n("gS")
    wg_done = P.n("wG")
    CW = [("gS", consts_g)]

    state = {
        "negm_guard": [0] * 8,
        "it": 0,
        "sq_guard": [0, 0],
        "ln_tm_guard": [0, 0],
        "lacc_guard": [0, 0],
        "diag_guard": [0] * 5,
    }

    def xcol(t):
        return 1024 * t

    def lncol(c, j=0):
        return 2048 * c + j

    def tT(out_ap, in_ap, waits=(), inc=None):
        return P.emit("t", "matmul", (out_ap, in_ap, identb[:]),
                      {"start": True, "stop": True}, waits=waits, inc=inc)

    def tmark():
        P.cnt["tC"] += 1
        meth, args, kw, waits, inc, incby = P.ops["t"][-2]
        assert meth == "matmul" and inc is None, (meth, inc)
        P.ops["t"][-2] = (meth, args, kw, waits, "tC", 1)
        return P.cnt["tC"]

    def emit_layernorm(dep_waits_fn):
        state["lnbank0"] = 0
        state["lnbank1"] = 0
        for t in range(NT):
            sl = 16 * (t % 2)
            lnb = ln_tm if t % 2 == 0 else ln_tm2

            def S(c):
                return stats[:, sl + c:sl + c + 1]

            xw = list(dep_waits_fn(t))
            xt = x_sb[:, xcol(t):xcol(t) + 1024]
            sq_w = xw + ([("vS", state["sq_guard"][t % 2])]
                         if state["sq_guard"][t % 2] else [])
            P.emit("s", "activation", (dump[:], xt, AF.Square),
                   {"accum_out": S(1)}, waits=sq_w, inc="sS")
            ss_done = P.n("sS")
            P.emit("v", "tensor_reduce", (S(0), xt, AX, OP.add),
                   {"negate": True}, waits=xw, inc="vS")
            P.emit("v", "tensor_scalar", (S(2), S(0), 1.0 / D, None, OP.mult))
            P.emit("v", "tensor_scalar", (S(3), S(1), 1.0 / D, None, OP.mult),
                   waits=[("sS", ss_done)], inc="vS")
            state["sq_guard"][t % 2] = P.n("vS")
            P.emit("v", "tensor_tensor", (S(4), S(2), S(2), OP.mult))
            P.emit("v", "tensor_tensor", (S(5), S(3), S(4), OP.subtract))
            P.emit("v", "tensor_scalar", (S(6), S(5), EPS, None, OP.add))
            P.emit("v", "reciprocal", (S(7), S(6)), inc="vS")
            rv_done = P.n("vS")
            P.emit("s", "activation", (S(8), S(7), AF.Sqrt),
                   waits=[("vS", rv_done)], inc="sS")
            rs_done = P.n("sS")
            lw = [("sS", rs_done)]
            if state["ln_tm_guard"][t % 2]:
                lw.append(("tC", state["ln_tm_guard"][t % 2]))
            P.emit("v", "tensor_scalar", (lnb[:], xt, S(2), S(8),
                                          OP.add, OP.mult),
                   waits=lw, inc="vS")
            lnt_done = P.n("vS")
            for b4 in range(4):
                seti = b4 % 2
                bank = psPT[seti]
                bw = CW + [("vS", lnt_done)]
                g = state[f"lnbank{seti}"]
                if g:
                    bw.append(g)
                for j in range(2):
                    c = 2 * b4 + j
                    tT(bank[:, 512 * j:512 * j + 128],
                       lnb[:, 128 * c:128 * c + 128],
                       waits=bw if j == 0 else ())
                tr_done = tmark()
                src = bass.AP(bank, 0, [[1024, 128], [512, 2], [1, 128]])
                out_ap = bass.AP(ln_sb, 2048 * 2 * b4 + 128 * t,
                                 [[DCH * 2048, 128], [2048, 2], [1, 128]])
                eng = "v" if b4 < 2 else "s"
                if eng == "v":
                    P.emit("v", "tensor_copy", (out_ap, src),
                           waits=[("tC", tr_done)], inc="vS")
                    state[f"lnbank{seti}"] = ("vS", P.n("vS"))
                else:
                    P.emit("s", "copy", (out_ap, src),
                           waits=[("tC", tr_done)], inc="sS")
                    state[f"lnbank{seti}"] = ("sS", P.n("sS"))
                state["ln_tm_guard"][t % 2] = tr_done
        state["ln_done_s"] = P.n("sS")
        return P.n("vS")

    def emit_xload(seq):
        xv = x_hbm.ap()[seq * T:(seq + 1) * T, :].rearrange("(t p) c -> p t c", p=128)
        xs = x_sb[:].rearrange("p (t c) -> p t c", c=1024)
        for g2 in range(2):
            P.emit("y", "dma_start", (xs[:, 8 * g2:8 * g2 + 8, :], xv[:, 8 * g2:8 * g2 + 8, :]),
                   inc="dX", incby=16)
        state["xload"] = P.n("dX")

    def emit_store(seq, half, prefetch_next=False):
        yv = y_hbm.ap()[seq * T + 1024 * half:seq * T + 1024 * half + 1024, :]
        yv = yv.rearrange("(t p) c -> p t c", p=128)
        xs = x_sb[:].rearrange("p (t c) -> p t c", c=1024)
        P.emit("y", "dma_start", (yv, xs[:, 8 * half:8 * half + 8, :]),
               waits=[("vS", P.n("vS"))], inc="dO", incby=16)
        if prefetch_next and seq + 1 < SEQS:
            nxt = seq + 1
            xv = x_hbm.ap()[nxt * T:(nxt + 1) * T, :].rearrange(
                "(t p) c -> p t c", p=128)
            P.emit("y", "dma_start",
                   (xs[:, 8 * half:8 * half + 8, :], xv[:, 8 * half:8 * half + 8, :]),
                   waits=[("dO", P.n("dO"))], inc="dX", incby=16)
            state["xload"] = P.n("dX")

    def emit_attention(seq, level=4):
        xload = state["xload"]

        if "attw" not in DEBUG_SKIP:
            P.emit("g", "dma_start", (wq_sb[:], wq_h.ap()), waits=CW, inc="wA", incby=16)
            P.emit("g", "dma_start", (wk_sb[:], wk_h.ap()), inc="wA", incby=16)
            P.emit("g", "dma_start", (wv_sb[:], wv_h.ap()), inc="wA", incby=16)
        wqkv_done = ("wA", P.n("wA"))
        if "attw" not in DEBUG_SKIP:
            P.emit("g", "dma_start", (wo_sb[:], wo_h.ap()), inc="wB", incby=16)
        wo_done = ("wB", P.n("wB"))

        xw = [("dX", xload)]
        ln_done = emit_layernorm(lambda t: xw)
        if level < 2:
            return

        mmw = [("vS", ln_done), ("sS", state["ln_done_s"]), wqkv_done]
        ev_guard = [0, 0]
        for qc in range(0 if "qproj" not in DEBUG_SKIP else 0, DCH if "qproj" not in DEBUG_SKIP else 0):
            for ng in range(4):
                bi = (qc * 4 + ng) % 2
                bank = psS[bi]
                bw = list(mmw)
                if ev_guard[bi]:
                    bw.append(("sS", ev_guard[bi]))
                for jd in range(DCH):
                    P.emit("t", "matmul",
                           (bank[:],
                            wq_sb[:, 1024 * qc + 128 * jd:1024 * qc + 128 * jd + 128],
                            ln_sb[:, lncol(jd, 512 * ng):lncol(jd, 512 * ng) + 512]),
                           {"start": jd == 0, "stop": jd == DCH - 1},
                           waits=bw if jd == 0 else ())
                mm_done = tmark()
                P.emit("s", "copy",
                       (q_sb[:, 2048 * qc + 512 * ng:2048 * qc + 512 * ng + 512], bank[:]),
                       waits=[("tC", mm_done)], inc="sS")
                ev_guard[bi] = P.n("sS")
        for kc in range(2 if "kproj" not in DEBUG_SKIP else 0):
            for ng in range(4):
                bi = (kc * 4 + ng) % 2
                bank = psS[bi]
                bw = [("sS", ev_guard[bi])]
                for jd in range(DCH):
                    P.emit("t", "matmul",
                           (bank[:],
                            wk_sb[:, 1024 * kc + 128 * jd:1024 * kc + 128 * jd + 128],
                            ln_sb[:, lncol(jd, 512 * ng):lncol(jd, 512 * ng) + 512]),
                           {"start": jd == 0, "stop": jd == DCH - 1},
                           waits=bw if jd == 0 else ())
                mm_done = tmark()
                P.emit("s", "copy",
                       (k_sb[:, 2048 * kc + 512 * ng:2048 * kc + 512 * ng + 512], bank[:]),
                       waits=[("tC", mm_done)], inc="sS")
                ev_guard[bi] = P.n("sS")
        v_guard = [0, 0]
        for vt in range(NT if "vproj" not in DEBUG_SKIP else 0):
            bank = psO[vt % 2]
            bw = [("vS", v_guard[vt % 2])] if v_guard[vt % 2] else list(mmw)
            for jd in range(DCH):
                P.emit("t", "matmul",
                       (bank[:, 0:256],
                        ln_sb[:, lncol(jd, 128 * vt):lncol(jd, 128 * vt) + 128],
                        wv_sb[:, 256 * jd:256 * jd + 256]),
                       {"start": jd == 0, "stop": jd == DCH - 1},
                       waits=bw if jd == 0 else ())
            mm_done = tmark()
            P.emit("v", "tensor_copy", (v_sb[:, 256 * vt:256 * vt + 256], bank[:, 0:256]),
                   waits=[("tC", mm_done)], inc="vS")
            v_guard[vt % 2] = P.n("vS")
        v_all = P.n("vS")
        if level < 3:
            return

        sbank_guard = [P.n("sS"), P.n("sS")]
        ptbank_guard = [0, 0]
        po_guard = [v_all, v_all]

        def stageA(c):
            it, qi = c["it"], c["qi"]
            nk = c["nk"]
            ngroups = (nk + 511) // 512
            c["ngroups"] = ngroups
            glast = ngroups - 1
            order = [glast] + list(range(glast))
            negs = it % 8
            pb = [p_sb, p2_sb, p3_sb, p4_sb, p5_sb][it % 5]
            c["pb"] = pb
            lset = it % 2
            lbase = 10 if lset == 0 else 26
            lsumc = 14 if lset == 0 else 30
            linvc = 15 if lset == 0 else 31
            m_done = 0
            for gi, g in enumerate(order):
                n = min(512, nk - 512 * g)
                bi = gi % 2
                bank = psS[bi]
                P.emit("t", "matmul",
                       (bank[:, 0:n],
                        q_sb[c["po_h"]:c["po_h"] + 64,
                             2048 * c["qc_h"] + 128 * qi:2048 * c["qc_h"] + 128 * qi + 128],
                        k_sb[c["ko_h"]:c["ko_h"] + 64,
                             2048 * c["kc_h"] + 512 * g:2048 * c["kc_h"] + 512 * g + n]),
                       {"start": True, "stop": True},
                       waits=[("sS", sbank_guard[bi])])
                s_done = tmark()
                if gi == 0:
                    doff = (nk - 128) - 512 * g
                    mw = [("tC", s_done)]
                    if state["negm_guard"][negs]:
                        mw.append(("sS", state["negm_guard"][negs]))
                    P.emit("v", "tensor_tensor",
                           (bank[:, doff:doff + 128], bank[:, doff:doff + 128],
                            tril[:], OP.add), waits=mw)
                    P.emit("v", "tensor_reduce",
                           (negm[:, negs:negs + 1], bank[:, doff:doff + 128], AX, OP.max),
                           {"negate": True}, inc="vS")
                    m_done = P.n("vS")
                ew = [("tC", s_done), ("vS", m_done)]
                if gi == 0 and state["lacc_guard"][lset]:
                    ew.append(("vS", state["lacc_guard"][lset]))
                P.emit("s", "activation",
                       (pb[:, 512 * g:512 * g + n], bank[:, 0:n], AF.Exp),
                       {"bias": negm[:, negs:negs + 1], "scale": 0.125,
                        "accum_out": stats[:, lbase + gi:lbase + gi + 1]},
                       waits=ew, inc="sS")
                sbank_guard[bi] = P.n("sS")
            state["negm_guard"][negs] = P.n("sS")
            exp_all = P.n("sS")
            P.emit("v", "tensor_reduce",
                   (stats[:, lsumc:lsumc + 1], stats[:, lbase:lbase + ngroups], AX, OP.add),
                   waits=[("sS", exp_all)], inc="vS")
            state["lacc_guard"][lset] = P.n("vS")
            P.emit("v", "reciprocal", (stats[:, linvc:linvc + 1], stats[:, lsumc:lsumc + 1]))
            dslot = it % 5
            dw = [("gS", consts_g)]
            if state["diag_guard"][dslot]:
                dw.append(("tC", state["diag_guard"][dslot]))
            P.emit("v", "tensor_scalar",
                   (diag[:, 128 * dslot:128 * dslot + 128], identb[:],
                    stats[:, linvc:linvc + 1], None, OP.mult),
                   waits=dw, inc="vS")
            c["dslot"] = dslot
            c["scale_done"] = P.n("vS")

        def stageB(c):
            if c is None:
                return
            it, qi, kv = c["it"], c["qi"], c["kv"]
            pb = c["pb"]
            nbat = (qi + 1 + 1) // 2
            evac_v = []
            for b in range(nbat):
                k0, k1 = 2 * b, min(2 * b + 2, qi + 1)
                seti = b % 2
                bank = psPT[seti]
                bw = [("vS", c["scale_done"])]
                if ptbank_guard[seti]:
                    bw.append(ptbank_guard[seti])
                for kt in range(k0, k1):
                    P.emit("t", "matmul",
                           (bank[:, 512 * (kt - k0):512 * (kt - k0) + 128],
                            pb[:, 128 * kt:128 * kt + 128],
                            diag[:, 128 * c["dslot"]:128 * c["dslot"] + 128]),
                           {"start": True, "stop": True},
                           waits=bw if kt == k0 else ())
                tr_done = tmark()
                state["diag_guard"][c["dslot"]] = tr_done
                src = bass.AP(bank, 0, [[1024, 128], [512, k1 - k0], [1, 128]])
                if b % 2 == 0:
                    P.emit("v", "tensor_copy",
                           (pt_sb[:, 128 * k0:128 * k1], src),
                           waits=[("tC", tr_done)], inc="vS")
                    ptbank_guard[seti] = ("vS", P.n("vS"))
                    evac_v.append((k0, k1, ("vS", P.n("vS"))))
                else:
                    P.emit("s", "copy",
                           (pt_sb[:, 128 * k0:128 * k1], src),
                           waits=[("tC", tr_done)], inc="sS")
                    ptbank_guard[seti] = ("sS", P.n("sS"))
                    evac_v.append((k0, k1, ("sS", P.n("sS"))))
            po_i = it % 2
            for (k0, k1, ev) in evac_v:
                for kt in range(k0, k1):
                    bw = [ev]
                    if kt == 0:
                        bw.append(("vS", po_guard[po_i]))
                    P.emit("t", "matmul",
                           (psO[po_i][0:64, 0:128],
                            v_sb[:, 256 * kt + 64 * kv:256 * kt + 64 * kv + 64],
                            pt_sb[:, 128 * kt:128 * kt + 128]),
                           {"start": kt == 0, "stop": kt == qi},
                           waits=bw)
            av_done = tmark()
            P.emit("v", "tensor_copy",
                   (q_sb[c["po_h"]:c["po_h"] + 64,
                         2048 * c["qc_h"] + 128 * qi:2048 * c["qc_h"] + 128 * qi + 128],
                    psO[po_i][0:64, 0:128]),
                   waits=[("tC", av_done)], inc="vS")
            po_guard[po_i] = P.n("vS")

        from collections import deque
        pend = deque()
        DEPTH = 4
        for h in range(HQ if DEBUG_HQ is None else DEBUG_HQ):
            kv = h // 4
            if h in HEADS_A:
                qc_h, po_h = HEADS_A.index(h), 0
            else:
                qc_h, po_h = HEADS_B.index(h), 64
            kc_h, ko_h = kv // 2, 64 * (kv % 2)
            for qi in range(NT if DEBUG_NQI is None else DEBUG_NQI):
                it = state["it"]
                state["it"] += 1
                c = {"it": it, "qi": qi, "kv": kv, "qc_h": qc_h, "po_h": po_h,
                     "kc_h": kc_h, "ko_h": ko_h, "nk": 128 * (qi + 1)}
                stageA(c)
                pend.append(c)
                if len(pend) > DEPTH:
                    stageB(pend.popleft())
        while pend:
            stageB(pend.popleft())

        ctx_done = P.n("vS")
        if level < 4:
            return

        out_guard = [None, None, None, None]
        for qi in range(NT):
            pair = psS if qi % 2 == 0 else psO
            for nh in range(2):
                bank = pair[nh]
                gidx = (0 if qi % 2 == 0 else 2) + nh
                bw = [("vS", ctx_done), wo_done]
                if qi % 2 == 0:
                    bw.append(("sS", sbank_guard[nh]))
                if out_guard[gidx]:
                    bw.append(out_guard[gidx])
                for cc in range(DCH):
                    P.emit("t", "matmul",
                           (bank[:],
                            q_sb[:, 2048 * cc + 128 * qi:2048 * cc + 128 * qi + 128],
                            wo_sb[:, 1024 * cc + 512 * nh:1024 * cc + 512 * nh + 512]),
                           {"start": cc == 0, "stop": cc == DCH - 1},
                           waits=bw if cc == 0 else ())
                mm_done = tmark()
                P.emit("v", "tensor_add",
                       (x_sb[:, xcol(qi) + 512 * nh:xcol(qi) + 512 * nh + 512],
                        x_sb[:, xcol(qi) + 512 * nh:xcol(qi) + 512 * nh + 512],
                        bank[:]),
                       waits=[("tC", mm_done)], inc="vS")
                out_guard[gidx] = ("vS", P.n("vS"))

    def emit_ffn(seq):
        xw = [("dX", state["xload"])]
        ln_done = emit_layernorm(lambda t: xw)
        g_guard = [0, 0]
        for t in range(NT):
            bank = psS[t % 2]
            bw = [("vS", ln_done), ("sS", state["ln_done_s"]), ("wG", wg_done)]
            if g_guard[t % 2]:
                bw.append(("vS", g_guard[t % 2]))
            for jd in range(DCH):
                P.emit("t", "matmul",
                       (bank[:, 0:8],
                        ln_sb[:, lncol(jd, 128 * t):lncol(jd, 128 * t) + 128],
                        wgate[:, 8 * jd:8 * jd + 8]),
                       {"start": jd == 0, "stop": jd == DCH - 1},
                       waits=bw if jd == 0 else ())
            mm_done = tmark()
            P.emit("v", "tensor_copy", (gsc[:, 0:8], bank[:, 0:8]),
                   waits=[("tC", mm_done)], inc="vS")
            g_guard[t % 2] = P.n("vS")
            P.emit("v", "tensor_reduce", (gsc[:, 8:9], gsc[:, 0:8], AX, OP.max),
                   {"negate": True})
            P.emit("v", "tensor_scalar", (gsc[:, 16:24], gsc[:, 0:8], gsc[:, 8:9], None, OP.add))
            P.emit("v", "tensor_scalar", (gsc[:, 24:32], gsc[:, 16:24], 0.0, None, OP.is_equal))
            P.emit("v", "scalar_tensor_tensor",
                   (gsc[:, 32:40], gsc[:, 24:32], NEGBIG, gsc[:, 0:8], OP.mult, OP.add))
            P.emit("v", "tensor_reduce", (gsc[:, 9:10], gsc[:, 32:40], AX, OP.max),
                   {"negate": True})
            P.emit("v", "tensor_tensor", (gsc[:, 10:11], gsc[:, 9:10], gsc[:, 8:9], OP.subtract),
                   inc="vS")
            pre_s = P.n("vS")
            P.emit("s", "activation", (gsc[:, 11:12], gsc[:, 10:11], AF.Exp),
                   {"scale": -1.0}, waits=[("vS", pre_s)], inc="sS")
            e_done = P.n("sS")
            P.emit("v", "tensor_scalar", (gsc[:, 12:13], gsc[:, 11:12], 1.0, None, OP.add),
                   waits=[("sS", e_done)])
            P.emit("v", "reciprocal", (gsc[:, 13:14], gsc[:, 12:13]), inc="vS")
            rd_done = P.n("vS")
            P.emit("s", "activation", (gsc[:, 40:48], gsc[:, 0:8], AF.Exp),
                   {"bias": gsc[:, 8:9]}, inc="sS")
            e1_done = P.n("sS")
            P.emit("v", "tensor_scalar", (gsc[:, 16:24], gsc[:, 0:8], gsc[:, 9:10], None, OP.add))
            P.emit("v", "tensor_scalar", (gsc[:, 24:32], gsc[:, 16:24], 0.0, None, OP.is_ge))
            P.emit("v", "scalar_tensor_tensor",
                   (coef[:, 8 * t:8 * t + 8], gsc[:, 40:48], gsc[:, 13:14], gsc[:, 24:32],
                    OP.mult, OP.mult),
                   waits=[("sS", e1_done), ("vS", rd_done)], inc="vS")
        gate_done = P.n("vS")

        gu_guard = [0, 0, 0, 0]
        sg_guard = [0, 0]
        y_guard = [0, 0]
        w1_guard = [0, 0]
        w3_guard = [0, 0]
        w2_guard = 0
        gu_sets = [psS[0], psS[1], psO[0], psO[1]]
        for half in range(2):
            for e in range(E):
                w_done = {}

                def emit_w13(grp):
                    sh = grp % 2
                    s1 = "u1a" if sh == 0 else "u1b"
                    s3 = "u3a" if sh == 0 else "u3b"
                    bw = [("tC", w1_guard[sh])] if w1_guard[sh] else []
                    P.emit("g", "dma_start",
                           (w1_sb[:, 4096 * sh:4096 * sh + 4096],
                            w1_h.ap()[e, :, 4096 * grp:4096 * grp + 4096]),
                           waits=bw, inc=s1, incby=16)
                    w_done[("w1", grp)] = (s1, P.n(s1))
                    bw = [("tC", w3_guard[sh])] if w3_guard[sh] else []
                    P.emit("g", "dma_start",
                           (w3_sb[:, 4096 * sh:4096 * sh + 4096],
                            w3_h.ap()[e, :, 4096 * grp:4096 * grp + 4096]),
                           waits=bw, inc=s3, incby=16)
                    w_done[("w3", grp)] = (s3, P.n(s3))

                emit_w13(0)
                emit_w13(1)
                w2w = [("tC", w2_guard)] if w2_guard else []
                for grp in range(4):
                    P.emit("g", "dma_start",
                           (w2_sb[:, 4096 * grp:4096 * grp + 4096],
                            w2_h.ap()[e, :, 4096 * grp:4096 * grp + 4096]),
                           waits=w2w if grp == 0 else (), inc="u2", incby=16)
                w2_done = ("u2", P.n("u2"))

                gu_idx = 0
                last_slot_use = {0: 0, 1: 0}
                for hc in range(HCH):
                    sh = (hc // 4) % 2
                    scol = 4096 * sh + 1024 * (hc % 4)
                    grp = hc // 4
                    for tg in range(2):
                        tok0 = 1024 * half + 512 * tg
                        pg = gu_sets[gu_idx % 4]
                        pu = gu_sets[(gu_idx + 1) % 4]
                        bw_g = [w_done[("w1", grp)], ("vS", gate_done)]
                        if gu_guard[gu_idx % 4]:
                            bw_g.append(("vS", gu_guard[gu_idx % 4]))
                        for jd in range(DCH):
                            P.emit("t", "matmul",
                                   (pg[:],
                                    w1_sb[:, scol + 128 * jd:scol + 128 * jd + 128],
                                    ln_sb[:, lncol(jd, tok0):lncol(jd, tok0) + 512]),
                                   {"start": jd == 0, "stop": jd == DCH - 1},
                                   waits=bw_g if jd == 0 else ())
                        g_done = tmark()
                        bw_u = [w_done[("w3", grp)]]
                        if gu_guard[(gu_idx + 1) % 4]:
                            bw_u.append(("vS", gu_guard[(gu_idx + 1) % 4]))
                        for jd in range(DCH):
                            P.emit("t", "matmul",
                                   (pu[:],
                                    w3_sb[:, scol + 128 * jd:scol + 128 * jd + 128],
                                    ln_sb[:, lncol(jd, tok0):lncol(jd, tok0) + 512]),
                                   {"start": jd == 0, "stop": jd == DCH - 1},
                                   waits=bw_u if jd == 0 else ())
                        u_done = tmark()
                        last_slot_use[sh] = u_done
                        ss = gu_idx % 2
                        sw = [("tC", g_done)]
                        if sg_guard[ss]:
                            sw.append(("vS", sg_guard[ss]))
                        P.emit("s", "activation",
                               (sg_sb[:, 512 * ss:512 * ss + 512], pg[:], AF.Silu),
                               waits=sw, inc="sS")
                        silu_done = P.n("sS")
                        hcol = 1024 * hc + 512 * tg
                        P.emit("v", "tensor_tensor",
                               (h_sb[:, hcol:hcol + 512], sg_sb[:, 512 * ss:512 * ss + 512],
                                pu[:], OP.mult),
                               waits=[("sS", silu_done), ("tC", u_done)], inc="vS")
                        gu_guard[gu_idx % 4] = P.n("vS")
                        gu_guard[(gu_idx + 1) % 4] = P.n("vS")
                        sg_guard[ss] = P.n("vS")
                        gu_idx += 2
                    if hc % 4 == 3:
                        w1_guard[sh] = last_slot_use[sh]
                        w3_guard[sh] = last_slot_use[sh]
                        nxt = hc // 4 + 2
                        if nxt < 4:
                            emit_w13(nxt)
                h_done = P.n("vS")

                for tg in range(2):
                    for j2 in range(4):
                        jh = 4 * tg + j2
                        tseq = 8 * half + jh
                        yset = j2 % 2
                        ybank = psPT[yset]
                        bw = [("vS", h_done), w2_done]
                        if y_guard[yset]:
                            bw.append(("vS", y_guard[yset]))
                        for hc in range(HCH):
                            for nh in range(2):
                                P.emit("t", "matmul",
                                       (ybank[:, 512 * nh:512 * nh + 512],
                                        h_sb[:, 1024 * hc + 128 * jh:1024 * hc + 128 * jh + 128],
                                        w2_sb[:, 1024 * hc + 512 * nh:1024 * hc + 512 * nh + 512]),
                                       {"start": hc == 0, "stop": hc == HCH - 1},
                                       waits=bw if (hc == 0 and nh == 0) else ())
                        y_done = tmark()
                        for nh in range(2):
                            xa = x_sb[:, xcol(tseq) + 512 * nh:xcol(tseq) + 512 * nh + 512]
                            P.emit("v", "scalar_tensor_tensor",
                                   (xa, ybank[:, 512 * nh:512 * nh + 512],
                                    coef[:, 8 * tseq + e:8 * tseq + e + 1],
                                    xa, OP.mult, OP.add),
                                   waits=[("tC", y_done)] if nh == 0 else (),
                                   inc="vS" if nh == 1 else None)
                        y_guard[yset] = P.n("vS")
                w2_guard = P.n("tC")
            emit_store(seq, half, prefetch_next=("attn" in str(phases)))

    attn_level = 0
    for ph in phases:
        if ph == "attn":
            attn_level = 4
        elif ph.startswith("attn"):
            attn_level = int(ph[4:])
    for seq in range(SEQS):
        if seq == 0 or not ("ffn" in phases and attn_level):
            emit_xload(seq)
        if attn_level:
            emit_attention(seq, attn_level)
            P.barrier()
        if "ffn" in phases:
            emit_ffn(seq)
            P.barrier()
        else:
            P.emit("v", "nop", inc="vS")
            for half in range(2):
                emit_store(seq, half)
            P.barrier()
    P.emit("y", None, waits=[("dO", P.n("dO"))])

    semh = {s: nc.alloc_semaphore(s) for s in SEMS}
    with nc.Block() as block:

        def runner(ops):
            def run(hdl):
                for (method, args, kwargs, waits, inc, incby) in ops:
                    for snm, val in waits:
                        hdl.wait_ge(semh[snm], val)
                    if method is None:
                        continue
                    ins = getattr(hdl, method)(*args, **kwargs)
                    if inc is not None:
                        ins.then_inc(semh[inc], incby)
            return run

        block.tensor(runner(P.ops["t"]))
        block.vector(runner(P.ops["v"]))
        block.scalar(runner(P.ops["s"]))
        block.gpsimd(runner(P.ops["g"]))
        block.sync(runner(P.ops["y"]))

    return nc


_CACHE = {}


def kernel(L0_attn_x, L0_attn_Wq, L0_attn_Wk, L0_attn_Wv, L0_attn_Wo,
           L0_ffn_w_gate, L0_ffn_w1, L0_ffn_w3, L0_ffn_w2):
    x = np.ascontiguousarray(np.asarray(L0_attn_x, dtype=np.float32))
    Wq = np.asarray(L0_attn_Wq, dtype=np.float32)
    Wk = np.asarray(L0_attn_Wk, dtype=np.float32)
    Wv = np.asarray(L0_attn_Wv, dtype=np.float32)
    Wo = np.asarray(L0_attn_Wo, dtype=np.float32)
    wg = np.asarray(L0_ffn_w_gate, dtype=np.float32)
    w1 = np.asarray(L0_ffn_w1, dtype=np.float32)
    w3 = np.asarray(L0_ffn_w3, dtype=np.float32)
    w2 = np.asarray(L0_ffn_w2, dtype=np.float32)

    colperm = np.concatenate([np.arange(64 * hh, 64 * hh + 64) for hh in HEAD_PERM])
    Wq = Wq[:, colperm]
    Wo = Wo[colperm, :]
    WqH = np.ascontiguousarray(
        Wq.reshape(8, 128, 8, 128).transpose(1, 2, 0, 3).reshape(128, 8192))
    WkH = np.ascontiguousarray(
        Wk.reshape(8, 128, 2, 128).transpose(1, 2, 0, 3).reshape(128, 2048))
    WvH = np.ascontiguousarray(
        Wv.reshape(8, 128, 256).transpose(1, 0, 2).reshape(128, 2048))
    WoH = np.ascontiguousarray(
        Wo.reshape(8, 128, 1024).transpose(1, 0, 2).reshape(128, 8192))
    WgH = np.ascontiguousarray(
        wg.reshape(8, 128, 8).transpose(1, 0, 2).reshape(128, 64))
    W1H = np.ascontiguousarray(
        w1.reshape(E, 8, 128, 16, 128).transpose(0, 2, 3, 1, 4).reshape(E, 128, 16384))
    W3H = np.ascontiguousarray(
        w3.reshape(E, 8, 128, 16, 128).transpose(0, 2, 3, 1, 4).reshape(E, 128, 16384))
    W2H = np.ascontiguousarray(
        w2.reshape(E, 16, 128, 1024).transpose(0, 2, 1, 3).reshape(E, 128, 16384))

    key = tuple(PHASES)
    if key not in _CACHE:
        _CACHE[key] = build_program()
    nc = _CACHE[key]

    in_maps = []
    for c in range(NCORES):
        xc = np.ascontiguousarray(x[2 * c:2 * c + 2].reshape(SEQS * T, D))
        in_maps.append({
            "xin": xc, "wq": WqH, "wk": WkH, "wv": WvH, "wo": WoH,
            "wg": WgH, "w1": W1H, "w3": W3H, "w2": W2H,
        })
    res = run_bass_kernel_spmd(nc, in_maps, core_ids=list(range(NCORES)))
    out = np.concatenate(
        [res.results[c]["yout"].reshape(SEQS, T, D) for c in range(NCORES)], axis=0)
    return out
